# version 02
# baseline (speedup 1.0000x reference)
import numpy as np
from contextlib import ExitStack
import concourse.bass as bass
import concourse.mybir as mybir
from concourse.bass_utils import run_bass_kernel_spmd
from concourse.alu_op_type import AluOpType as OP

F32 = mybir.dt.float32
BF16 = mybir.dt.bfloat16
I32 = mybir.dt.int32
AF = mybir.ActivationFunctionType
AX = mybir.AxisListType.X

T = 2048
CD = 2048
FI = 8192
NQ = 16
NKV = 4
NT = T // 128
NCC = CD // 128
NIC = FI // 128

DEBUG = False


class KB:
    ENGINES = ("sync", "gpsimd", "vector", "scalar", "tensor")

    def __init__(self):
        self.nc = bass.Bass("TRN2", target_bir_lowering=False)
        self.es = ExitStack()
        self.ops = {e: [] for e in self.ENGINES}
        self.sems = {}
        self.cnt = {}

    def sem(self, name):
        if name not in self.sems:
            self.sems[name] = self.es.enter_context(self.nc.semaphore(name))
            self.cnt[name] = 0
        return self.sems[name]

    def n(self, name):
        self.sem(name)
        return self.cnt[name]

    def mk(self, eng):
        def emit(meth, *args, w=(), i=None, **kw):
            waits = [(self.sem(s), v) for (s, v) in w if v > 0]
            ih = None
            amt = 0
            if i is not None:
                ih = self.sem(i[0])
                self.cnt[i[0]] += i[1]
                amt = i[1]
            def f(e, meth=meth, args=args, kw=kw, waits=waits, ih=ih, amt=amt):
                for s, v in waits:
                    e.wait_ge(s, v)
                r = getattr(e, meth)(*args, **kw)
                if ih is not None:
                    r.then_inc(ih, amt)
            self.ops[eng].append(f)
        return emit

    def sb(self, name, shape, dt, stack=None, side=None):
        return (stack or self.es).enter_context(
            self.nc.sbuf_tensor(name, shape, dt, side=side))

    def ps(self, name, shape, stack=None):
        return (stack or self.es).enter_context(self.nc.psum_tensor(name, shape, F32))


def build_program(debug=DEBUG, phases=6, b2lvl=3):
    kb = KB()
    nc = kb.nc
    TE, VE, SE, GQ, SQ = (kb.mk("tensor"), kb.mk("vector"), kb.mk("scalar"),
                          kb.mk("gpsimd"), kb.mk("sync"))

    def din(name, shape, dt):
        return nc.dram_tensor(name, shape, dt, kind="ExternalInput").ap()

    x_d = din("x", [T, CD], F32)
    wq_d = din("wq", [CD, CD], F32)
    wk_d = din("wk", [NKV * 128, CD], F32)
    wv_d = din("wv", [NKV * 128, CD], F32)
    wp_d = din("wp", [CD, CD], F32)
    gw_d = din("gw", [FI, CD], I32)
    uw_d = din("uw", [FI, CD], I32)
    dw_d = din("dw", [CD, FI], I32)
    gs_d = din("gs", [64, 128], F32)
    us_d = din("us", [64, 128], F32)
    ds_d = din("ds", [16, 128], F32)
    cq_d = din("cq", [128, T], F32)
    sq_d = din("sq", [128, T], F32)
    ck_d = din("ck", [128, T], F32)
    sk_d = din("sk", [128, T], F32)
    m01_d = din("m01", [128, 128], F32)
    id_d = din("ident", [128, 128], F32)

    out_d = nc.dram_tensor("out", [T, CD], F32, kind="ExternalOutput").ap()

    scr = "ExternalOutput" if debug else "Internal"
    xn_scr = nc.dram_tensor("xn_scr", [T, CD], BF16, kind=scr).ap()
    qT_scr = nc.dram_tensor("qT_scr", [CD, T], BF16, kind=scr).ap()
    h_scr = nc.dram_tensor("h_scr", [FI, T], BF16, kind=scr).ap()
    s_scr = nc.dram_tensor("s_scr", [T, CD], BF16, kind=scr).ap()
    wg_s = nc.dram_tensor("wg_s", [FI, CD], BF16, kind="Internal").ap()
    wu_s = nc.dram_tensor("wu_s", [FI, CD], BF16, kind="Internal").ap()
    wd_s = nc.dram_tensor("wd_s", [CD, FI], BF16, kind="Internal").ap()
    wq_s = nc.dram_tensor("wq_s", [CD, CD], BF16, kind="Internal").ap()
    wk_s = nc.dram_tensor("wk_s", [NKV * 128, CD], BF16, kind="Internal").ap()
    wv_s = nc.dram_tensor("wv_s", [NKV * 128, CD], BF16, kind="Internal").ap()
    wp_s = nc.dram_tensor("wp_s", [CD, CD], BF16, kind="Internal").ap()

    ident = kb.sb("ident_sb", [128, 128], F32)
    ones_bf = kb.sb("ones_bf", [128, 128], BF16)
    m01_bf = kb.sb("m01_bf", [128, 128], BF16)
    m01_f = kb.sb("m01_f", [128, 128], F32)
    cq_b = kb.sb("cq_b", [128, T], BF16)
    sq_b = kb.sb("sq_b", [128, T], BF16)
    ck_b = kb.sb("ck_b", [128, T], BF16)
    sk_b = kb.sb("sk_b", [128, T], BF16)
    gs_col = kb.sb("gs_col", [128, 64], F32)
    us_col = kb.sb("us_col", [128, 64], F32)
    ds_col = kb.sb("ds_col", [128, 16], F32)
    ds_negb = kb.sb("ds_negb", [128, 16], F32)
    eps_c = kb.sb("eps_c", [128, 1], F32)
    scale_nat = kb.sb("scale_nat", [128, 128], F32)

    p0 = ExitStack()
    tab_f = kb.sb("tab_f", [128, T], F32, p0)
    p0ps = kb.ps("p0ps", [128, 128], p0)

    GQ("dma_start", ident[:], id_d, i=("c_ld", 16))
    GQ("dma_start", m01_f[:], m01_d, i=("c_ld", 16))
    VE("memset", ones_bf[:], 1.0)
    VE("memset", eps_c[:], 1e-5)
    VE("tensor_copy", m01_bf[:], m01_f[:], w=[("c_ld", 32)], i=("s_dve", 1))
    for (src, dst) in ((cq_d, cq_b), (sq_d, sq_b), (ck_d, ck_b), (sk_d, sk_b)):
        GQ("dma_start", tab_f[:], src, w=[("s_dve", kb.n("s_dve"))],
           i=("c_tab", 16))
        VE("tensor_copy", dst[:], tab_f[:], w=[("c_tab", kb.n("c_tab"))],
           i=("s_dve", 1))
    for (src, rows, dst, ncols) in ((gs_d, 64, gs_col, 64), (us_d, 64, us_col, 64),
                                    (ds_d, 16, ds_col, 16)):
        GQ("dma_start", scale_nat[0:rows, :], src,
           w=[("s_dve", kb.n("s_dve"))], i=("c_sc", 16))
        TE("transpose", p0ps[:, 0:ncols], scale_nat[0:rows, :],
           ident[0:rows, 0:rows],
           w=[("c_sc", kb.n("c_sc")), ("s_dve", kb.n("s_dve"))], i=("p0t", 1))
        VE("tensor_copy", dst[:], p0ps[:, 0:ncols],
           w=[("p0t", kb.n("p0t"))], i=("s_dve", 1))
    VE("tensor_scalar_mul", ds_negb[:], ds_col[:], -128.0,
       w=[("s_dve", kb.n("s_dve"))], i=("s_dve", 1))
    P0_DVE = kb.n("s_dve")
    p0.close()

    pxn = ExitStack()
    xnT = kb.sb("xnT", [128, NCC * T], BF16, pxn)
    pa = ExitStack()
    x_ring = [kb.sb(f"x_r{p}", [128, CD], F32, pa) for p in range(2)]
    sq_ring = [kb.sb(f"sq_r{p}", [128, CD], F32, pa) for p in range(2)]
    xnb_ring = [kb.sb(f"xnb_r{p}", [128, CD], BF16, pa) for p in range(2)]
    c1_ring = [kb.sb(f"lc1_{p}", [128, 1], F32, pa) for p in range(2)]
    cm_ring = [kb.sb(f"lcm_{p}", [128, 1], F32, pa) for p in range(2)]
    cs_ring = [kb.sb(f"lcs_{p}", [128, 1], F32, pa) for p in range(2)]
    cd_ring = [kb.sb(f"lcd_{p}", [128, 1], F32, pa) for p in range(2)]
    cr_ring = [kb.sb(f"lcr_{p}", [128, 1], F32, pa) for p in range(2)]

    for tt in range(NT):
        p = tt % 2
        w = [("a_xn", tt - 1)]
        if tt < 2:
            w.append(("s_dve", P0_DVE))
        SQ("dma_start", x_ring[p][:], x_d[tt * 128:(tt + 1) * 128, :],
           w=w, i=("a_xld", 16))
        VE("reduce_sum", c1_ring[p][:], x_ring[p][:], axis=AX,
           w=[("a_xld", 16 * (tt + 1)), ("a_m", tt - 1)], i=("a_c1", 1))
        VE("tensor_scalar_mul", cm_ring[p][:], c1_ring[p][:], 1.0 / CD,
           w=[("a_c1", tt + 1), ("a_xn", tt - 1), ("a_sq", tt - 1)], i=("a_m", 1))
        SE("activation", sq_ring[p][:], x_ring[p][:], AF.Square,
           bias=cm_ring[p][:], scale=-1.0,
           w=[("a_m", tt + 1), ("a_ss", tt - 1)], i=("a_sq", 1))
        VE("reduce_sum", cs_ring[p][:], sq_ring[p][:], axis=AX,
           w=[("a_sq", tt + 1), ("a_std", tt - 1)], i=("a_ss", 1))
        SE("activation", cd_ring[p][:], cs_ring[p][:], AF.Sqrt,
           bias=eps_c[:], scale=1.0 / CD,
           w=[("a_ss", tt + 1), ("a_rc", tt - 1)], i=("a_std", 1))
        VE("reciprocal", cr_ring[p][:], cd_ring[p][:],
           w=[("a_std", tt + 1), ("a_xn", tt - 1)], i=("a_rc", 1))
        VE("tensor_scalar", xnb_ring[p][:], x_ring[p][:],
           cm_ring[p][:], cr_ring[p][:], OP.subtract, OP.mult,
           w=[("a_rc", tt + 1), ("a_xnst", (tt - 1) * 16)], i=("a_xn", 1))
        GQ("dma_start", xn_scr[tt * 128:(tt + 1) * 128, :], xnb_ring[p][:],
           w=[("a_xn", kb.n("a_xn"))], i=("a_xnst", 16))
    for cc in range(NCC):
        SQ("dma_start", xnT[:, cc * T:(cc + 1) * T],
           xn_scr[:, cc * 128:(cc + 1) * 128],
           w=[("a_xnst", 16 * NT)], i=("a_xnT", 16), transpose=True)
    A_XNT = kb.n("a_xnT")
    pa.close()

    if phases < 2:
        fin = [("a_xnT", kb.n("a_xnT")), ("a_xnst", kb.n("a_xnst"))]
        kb.ops["gpsimd"].append(lambda e, fy=[(kb.sem(s), v) for s, v in fin]:
                                [e.wait_ge(s, v) for s, v in fy])
        pxn.close()
        return _finish(kb, nc)
    pb1 = ExitStack()
    gu_ring = [kb.sb(f"gu_{r}", [128, 16 * 512], BF16, pb1) for r in range(4)]
    silu_ring = [kb.sb(f"sil_{r}", [128, CD], BF16, pb1) for r in range(2)]
    h_ring = [kb.sb(f"h_{r}", [128, CD], BF16, pb1) for r in range(2)]
    dwi_ring = [kb.sb(f"dwi_{r}", [128, 2048], I32, pb1) for r in range(2)]
    dwb_ring = [kb.sb(f"dwb_{r}", [128, 2048], BF16, pb1) for r in range(2)]
    g_ps = kb.ps("g_ps", [128, CD], pb1)
    u_ps = kb.ps("u_ps", [128, CD], pb1)

    def emit_gu_cast(sg):
        r0 = sg * 512
        GQ("dma_start", wg_s[r0:r0 + 512, :], gw_d[r0:r0 + 512, :], i=("stg_g", 16))
        GQ("dma_start", wu_s[r0:r0 + 512, :], uw_d[r0:r0 + 512, :], i=("stg_u", 16))

    def emit_attn_casts():
        for q in range(4):
            r0 = q * 512
            GQ("dma_start", wq_s[r0:r0 + 512, :], wq_d[r0:r0 + 512, :],
               i=("stg_q", 16))
            GQ("dma_start", wp_s[r0:r0 + 512, :], wp_d[r0:r0 + 512, :],
               i=("stg_p", 16))
        GQ("dma_start", wk_s, wk_d, i=("stg_k", 16))
        GQ("dma_start", wv_s, wv_d, i=("stg_v", 16))

    def emit_slab_reads(sg):
        for wi, (wsrc, stgsem) in enumerate(((wg_s, "stg_g"), (wu_s, "stg_u"))):
            k = 2 * sg + wi
            slot = gu_ring[k % 4]
            for cc in range(NCC):
                w = [(stgsem, 16 * (sg + 1))]
                if k >= 4:
                    w.append(("b1_umm", 4 * (k // 2) - 4))
                if k == 0 and cc == 0:
                    w.append(("a_xn", NT))
                    w.append(("a_xnst", 16 * NT))
                SQ("dma_start", slot[:, cc * 512:(cc + 1) * 512],
                   wsrc[sg * 512:(sg + 1) * 512, cc * 128:(cc + 1) * 128],
                   w=w, i=("b1_slab", 16), transpose=True)

    wd_tiles = [(oc, qq) for oc in range(16) for qq in range(4)]

    def emit_wd_load(idx):
        if idx >= 64:
            return
        oc, qq = wd_tiles[idx]
        SQ("dma_start", dwi_ring[idx % 2][:],
           dw_d[oc * 128:(oc + 1) * 128, qq * 2048:(qq + 1) * 2048],
           w=[("b1_wdst", (idx - 1) * 16)], i=("b1_wdld", 16))

    def emit_wd_compute(idx):
        oc, qq = wd_tiles[idx]
        p = idx % 2
        SE("activation", dwb_ring[p][:], dwi_ring[p][:], AF.Identity,
           bias=ds_negb[:, oc:oc + 1], scale=ds_col[:, oc:oc + 1],
           w=[("b1_wdld", 16 * (idx + 1)), ("b1_wdst", (idx - 1) * 16),
              ("s_dve", P0_DVE)],
           i=("b1_wddq", 1))
        GQ("dma_start", wd_s[oc * 128:(oc + 1) * 128, qq * 2048:(qq + 1) * 2048],
           dwb_ring[p][:], w=[("b1_wddq", kb.n("b1_wddq"))], i=("b1_wdst", 16))

    emit_gu_cast(0)
    emit_gu_cast(1)
    emit_slab_reads(0)
    emit_wd_load(0)
    emit_wd_load(1)

    for ic in range(NIC):
        sg = ic // 4
        io = ic % 4
        if io == 0:
            if sg + 1 < 16:
                emit_slab_reads(sg + 1)
            if sg + 2 < 16:
                emit_gu_cast(sg + 2)
            if sg == 8:
                emit_attn_casts()
        gset = gu_ring[(2 * sg) % 4]
        uset = gu_ring[(2 * sg + 1) % 4]
        w = [("b1_slab", 16 * 16 * (2 * sg + 1)), ("s_dve", P0_DVE),
             ("b1_sil", ic)]
        if ic == 0:
            w.append(("a_xnT", A_XNT))
        first = True
        for cc in range(NCC):
            for jc in range(4):
                TE("matmul", g_ps[:, jc * 512:(jc + 1) * 512],
                   gset[:, cc * 512 + io * 128: cc * 512 + io * 128 + 128],
                   xnT[:, cc * T + jc * 512: cc * T + (jc + 1) * 512],
                   start=(cc == 0), stop=(cc == NCC - 1),
                   w=(w if first else ()),
                   i=(("b1_gmm", 1) if (cc == NCC - 1 and jc == 3) else None))
                first = False
        w = [("b1_slab", 16 * 16 * (2 * sg + 2)), ("b1_h", ic)]
        first = True
        for cc in range(NCC):
            for jc in range(4):
                TE("matmul", u_ps[:, jc * 512:(jc + 1) * 512],
                   uset[:, cc * 512 + io * 128: cc * 512 + io * 128 + 128],
                   xnT[:, cc * T + jc * 512: cc * T + (jc + 1) * 512],
                   start=(cc == 0), stop=(cc == NCC - 1),
                   w=(w if first else ()),
                   i=(("b1_umm", 1) if (cc == NCC - 1 and jc == 3) else None))
                first = False
        SE("activation", silu_ring[ic % 2][:], g_ps[:], AF.Silu,
           scale=gs_col[:, ic:ic + 1],
           w=[("b1_gmm", ic + 1), ("b1_h", ic - 1)], i=("b1_sil", 1))
        emit_wd_compute(ic)
        emit_wd_load(ic + 2)
        VE("scalar_tensor_tensor", h_ring[ic % 2][:], u_ps[:],
           us_col[:, ic:ic + 1], silu_ring[ic % 2][:], OP.mult, OP.mult,
           w=[("b1_umm", ic + 1), ("b1_sil", ic + 1), ("b1_hst", (ic - 1) * 16)],
           i=("b1_h", 1))
        GQ("dma_start", h_scr[ic * 128:(ic + 1) * 128, :], h_ring[ic % 2][:],
           w=[("b1_h", kb.n("b1_h"))], i=("b1_hst", 16))
    pb1.close()

    if phases < 3:
        fin = [("b1_hst", kb.n("b1_hst")), ("b1_wdst", kb.n("b1_wdst"))]
        kb.ops["gpsimd"].append(lambda e, fy=[(kb.sem(s), v) for s, v in fin]:
                                [e.wait_ge(s, v) for s, v in fy])
        pxn.close()
        return _finish(kb, nc)
    pkv = ExitStack()
    KT = kb.sb("KT", [128, NKV * T], BF16, pkv, side="right")
    V_sb = kb.sb("V_sb", [128, NT * 512], BF16, pkv, side="right")
    pc1 = ExitStack()
    wsl_ring = [kb.sb(f"wsl_{r}", [128, 16 * 512], BF16, pc1) for r in range(2)]
    qh_ring = [kb.sb(f"qh_{r}", [128, T], BF16, pc1) for r in range(2)]
    tmpA = kb.sb("tmpA", [128, T], BF16, pc1)
    tmpB = kb.sb("tmpB", [128, T], BF16, pc1)
    qst_ring = [kb.sb(f"qst_{r}", [128, T], BF16, pc1) for r in range(2)]
    kq_ps = [kb.ps(f"kq_ps{r}", [128, CD], pc1) for r in range(2)]

    B1_DONE = [("b1_umm", NIC), ("b1_h", NIC), ("b1_hst", 16 * NIC)]

    def c1_slabs(ring_idx, wsrc, nrows, wname, stg, extra_w=()):
        for cc in range(NCC):
            w = [(stg, 16)] + list(extra_w if cc == 0 else [])
            SQ("dma_start", wsl_ring[ring_idx][:, cc * 512:(cc + 1) * 512],
               wsrc[0:nrows, cc * 128:(cc + 1) * 128],
               w=w, i=(wname, 16), transpose=True)

    c1_slabs(0, wk_s, 512, "c1_ksl", "stg_k", extra_w=B1_DONE)
    c1_slabs(1, wv_s, 512, "c1_vsl", "stg_v", extra_w=B1_DONE)

    def rope(dst_lo, dst_hi, src, cosb, sinb, iname, extra_w=()):
        d0 = kb.n("s_dve")
        VE("tensor_tensor", tmpA[0:64, :], src[0:64, :], cosb[0:64, :], OP.mult,
           w=[("s_dve", d0), (iname, kb.n(iname))] + list(extra_w),
           i=("s_dve", 1))
        VE("tensor_tensor", tmpB[0:64, :], src[64:128, :], sinb[64:128, :],
           OP.mult, i=("s_dve", 1))
        VE("tensor_tensor", tmpA[64:128, :], src[64:128, :], cosb[64:128, :],
           OP.mult, i=("s_dve", 1))
        VE("tensor_tensor", tmpB[64:128, :], src[0:64, :], sinb[0:64, :],
           OP.mult, i=("s_dve", 1))
        VE("tensor_tensor", dst_lo, tmpA[0:64, :], tmpB[0:64, :], OP.subtract,
           w=[("s_dve", kb.n("s_dve"))], i=("s_dve", 1))
        VE("tensor_tensor", dst_hi, tmpA[64:128, :], tmpB[64:128, :], OP.add,
           i=(iname, 1))

    for hk in range(NKV):
        p = hk % 2
        w = [("c1_ksl", 16 * 16), ("a_xnT", A_XNT), ("s_dve", P0_DVE)]
        if hk >= 2:
            w.append(("c1_qh", hk - 1))
        first = True
        for cc in range(NCC):
            for jc in range(4):
                TE("matmul", kq_ps[p][:, jc * 512:(jc + 1) * 512],
                   wsl_ring[0][:, cc * 512 + hk * 128: cc * 512 + hk * 128 + 128],
                   xnT[:, cc * T + jc * 512: cc * T + (jc + 1) * 512],
                   start=(cc == 0), stop=(cc == NCC - 1),
                   w=(w if first else ()),
                   i=(("c1_kmm", 1) if (cc == NCC - 1 and jc == 3) else None))
                first = False
        SE("activation", qh_ring[p][:], kq_ps[p][:], AF.Identity,
           w=[("c1_kmm", hk + 1), ("s_dve", kb.n("s_dve"))], i=("c1_qh", 1))
        rope(KT[0:64, hk * T:(hk + 1) * T], KT[64:128, hk * T:(hk + 1) * T],
             qh_ring[p], ck_b, sk_b, "c1_krope",
             extra_w=[("c1_qh", kb.n("c1_qh"))])

    for tt in range(NT):
        p = tt % 2
        w = [("c1_vsl", 16 * 16), ("c1_qh", NKV), ("c1_vev", tt - 1)]
        first = True
        for cc in range(NCC):
            TE("matmul", kq_ps[p][:, 0:512],
               xnT[:, cc * T + tt * 128: cc * T + tt * 128 + 128],
               wsl_ring[1][:, cc * 512:(cc + 1) * 512],
               start=(cc == 0), stop=(cc == NCC - 1),
               w=(w if first else ()),
               i=(("c1_vmm", 1) if cc == NCC - 1 else None))
            first = False
        SE("activation", V_sb[:, tt * 512:(tt + 1) * 512], kq_ps[p][:, 0:512],
           AF.Identity, w=[("c1_vmm", tt + 1)], i=("c1_vev", 1))

    for osl in range(4):
        for cc in range(NCC):
            w = [("stg_q", 16 * (osl + 1))]
            if osl == 0:
                w.append(("c1_kmm", NKV))
            elif osl == 1:
                w.append(("c1_vmm", NT))
            else:
                w.append(("c1_qmm", (osl - 2) * 4 + 4))
            SQ("dma_start", wsl_ring[osl % 2][:, cc * 512:(cc + 1) * 512],
               wq_s[osl * 512:(osl + 1) * 512, cc * 128:(cc + 1) * 128],
               w=w, i=("c1_qsl", 16), transpose=True)
        for hh in range(4):
            hq = osl * 4 + hh
            p = hq % 2
            w = [("c1_qsl", 16 * 16 * (osl + 1)), ("c1_vev", NT),
                 ("c1_qev", kb.n("c1_qev") - 1)]
            first = True
            for cc in range(NCC):
                for jc in range(4):
                    TE("matmul", kq_ps[p][:, jc * 512:(jc + 1) * 512],
                       wsl_ring[osl % 2][:, cc * 512 + hh * 128:
                                          cc * 512 + hh * 128 + 128],
                       xnT[:, cc * T + jc * 512: cc * T + (jc + 1) * 512],
                       start=(cc == 0), stop=(cc == NCC - 1),
                       w=(w if first else ()),
                       i=(("c1_qmm", 1) if (cc == NCC - 1 and jc == 3) else None))
                    first = False
            SE("activation", qh_ring[p][:], kq_ps[p][:], AF.Identity,
               w=[("c1_qmm", kb.n("c1_qmm")), ("s_dve", kb.n("s_dve"))],
               i=("c1_qev", 1))
            rope(qst_ring[p][0:64, :], qst_ring[p][64:128, :],
                 qh_ring[p], cq_b, sq_b, "c1_qrope",
                 extra_w=[("c1_qev", kb.n("c1_qev")),
                          ("c1_qst", (hq - 1) * 16)])
            GQ("dma_start", qT_scr[hq * 128:(hq + 1) * 128, :], qst_ring[p][:],
               w=[("c1_qrope", kb.n("c1_qrope"))], i=("c1_qst", 16))
    pc1.close()
    pxn.close()

    if phases < 4:
        fin = [("c1_qst", kb.n("c1_qst"))]
        kb.ops["gpsimd"].append(lambda e, fy=[(kb.sem(s), v) for s, v in fin]:
                                [e.wait_ge(s, v) for s, v in fy])
        pkv.close()
        return _finish(kb, nc)
    pyt = ExitStack()
    YT = kb.sb("YT", [128, NQ * T], BF16, pyt)
    pc2 = ExitStack()
    qr_ring = [kb.sb(f"qr_{r}", [128, T], BF16, pc2) for r in range(3)]
    E_ring = [kb.sb(f"E_{r}", [128, 512], BF16, pc2) for r in range(4)]
    et_ring = [kb.sb(f"et_{r}", [128, 128], BF16, pc2) for r in range(2)]
    rc_ring = [kb.sb(f"rc_{r}", [128, 512], F32, pc2) for r in range(2)]
    s_ps = [kb.ps(f"s_ps{r}", [128, 512], pc2) for r in range(2)]
    d_ps = [kb.ps(f"d_ps{r}", [128, 512], pc2) for r in range(2)]
    o_ps = [kb.ps(f"o_ps{r}", [128, 512], pc2) for r in range(2)]

    C1_DONE = [("c1_qst", 16 * NQ), ("c1_qmm", NQ), ("c1_qrope", NQ)]

    exp_done_at = {}
    smm_idx = 0
    njc_idx = 0

    for hq in range(NQ):
        kv = hq // 4
        w = [("c1_qst", 16 * (hq + 1))]
        if hq < 3:
            w += C1_DONE
        else:
            w.append(("c2_eo", (hq - 2) * 40))
        SQ("dma_start", qr_ring[hq % 3][:], qT_scr[hq * 128:(hq + 1) * 128, :],
           w=w, i=("c2_qld", 16))
        for jc in range(4):
            par = njc_idx % 2
            njs = 4 * jc + 4
            for js in range(njs):
                sp = smm_idx % 2
                w = [("c2_qld", 16 * (hq + 1))]
                if smm_idx >= 2:
                    pe = exp_done_at[smm_idx - 2]
                    w.append(("c2_eA", pe[0]))
                    if pe[1]:
                        w.append(("c2_eV", pe[1]))
                if smm_idx == 0:
                    w += [("c1_vev", NT), ("c1_krope", NKV), ("c1_qev", NQ)]
                TE("matmul", s_ps[sp][:],
                   KT[:, kv * T + js * 128: kv * T + js * 128 + 128],
                   qr_ring[hq % 3][:, jc * 512:(jc + 1) * 512],
                   start=True, stop=True, w=w, i=("c2_smm", 1))
                this_smm = kb.n("c2_smm")
                r = js - 4 * jc
                eslot = E_ring[smm_idx % 4]
                we = [("c2_smm", this_smm), ("c2_eo", smm_idx - 3)]
                if r < 0:
                    SE("activation", eslot[:], s_ps[sp][:], AF.Exp,
                       w=we, i=("c2_eA", 1))
                else:
                    if r > 0:
                        VE("memset", eslot[:, 0:128 * r], 0.0,
                           w=we + [("s_dve", kb.n("s_dve"))], i=("c2_eV", 1))
                    SE("activation", et_ring[smm_idx % 2][:],
                       s_ps[sp][:, 128 * r:128 * r + 128], AF.Exp,
                       w=we + [("c2_eV", kb.n("c2_eV"))], i=("c2_eA", 1))
                    if r < 3:
                        SE("activation", eslot[:, 128 * r + 128:512],
                           s_ps[sp][:, 128 * r + 128:512], AF.Exp,
                           i=("c2_eA", 1))
                    VE("tensor_tensor", eslot[:, 128 * r:128 * r + 128],
                       et_ring[smm_idx % 2][:], m01_bf[:], OP.mult,
                       w=[("c2_eA", kb.n("c2_eA")), ("s_dve", kb.n("s_dve"))],
                       i=("c2_eV", 1))
                exp_done_at[smm_idx] = (kb.n("c2_eA"), kb.n("c2_eV"))
                smm_idx += 1
                pe = exp_done_at[smm_idx - 1]
                w = [("c2_eA", pe[0])]
                if pe[1]:
                    w.append(("c2_eV", pe[1]))
                if js == 0:
                    w.append(("c2_yev", njc_idx - 1))
                TE("matmul", d_ps[par][:], ones_bf[:], eslot[:],
                   start=(js == 0), stop=(js == njs - 1), w=w)
                TE("matmul", o_ps[par][:],
                   V_sb[:, js * 512 + kv * 128: js * 512 + kv * 128 + 128],
                   eslot[:], start=(js == 0), stop=(js == njs - 1),
                   i=("c2_eo", 1))
            d0 = kb.n("s_dve")
            VE("reciprocal", rc_ring[par][:], d_ps[par][:],
               w=[("c2_eo", kb.n("c2_eo")), ("s_dve", d0)], i=("s_dve", 1))
            VE("tensor_tensor",
               YT[:, hq * T + jc * 512: hq * T + (jc + 1) * 512],
               o_ps[par][:], rc_ring[par][:], OP.mult,
               w=[("s_dve", kb.n("s_dve"))], i=("c2_yev", 1))
            njc_idx += 1
    pc2.close()
    pkv.close()
    if phases < 5:
        yt_scr = nc.dram_tensor("yt_scr", [CD, T], BF16,
                                kind="ExternalOutput" if debug else "Internal").ap()
        for hq in range(NQ):
            GQ("dma_start", yt_scr[hq * 128:(hq + 1) * 128, :].rearrange(
                   "p (n t) -> p n t", n=1),
               YT[:, hq * T:(hq + 1) * T].rearrange("p (n t) -> p n t", n=1),
               w=[("c2_yev", 64)], i=("dbg_yt", 16))
        fin = [("dbg_yt", kb.n("dbg_yt"))]
        kb.ops["gpsimd"].append(lambda e, fy=[(kb.sem(s), v) for s, v in fin]:
                                [e.wait_ge(s, v) for s, v in fy])
        pyt.close()
        return _finish(kb, nc)

    pc3 = ExitStack()
    wp_ring = [kb.sb(f"wpr_{r}", [128, 16 * 512], BF16, pc3) for r in range(2)]
    sst_ring = [kb.sb(f"sst_{r}", [128, 8 * 512], BF16, pc3) for r in range(2)]
    c3_ps = kb.ps("c3_ps", [128, 4096], pc3)

    C2_DONE = [("c2_yev", 64), ("c2_eo", kb.n("c2_eo")),
               ("c2_eV", kb.n("c2_eV"))]

    ph_idx = 0
    for oct_ in range(2):
        for ph in range(4):
            pr = ph_idx % 2
            for hd in range(NCC):
                w = [("stg_p", 64)]
                if ph_idx >= 2:
                    w.append(("c3_mm", ph_idx - 1))
                if ph_idx < 2 and hd == 0:
                    w += C2_DONE
                SQ("dma_start", wp_ring[pr][:, hd * 512:(hd + 1) * 512],
                   wp_s[ph * 512:(ph + 1) * 512, hd * 128:(hd + 1) * 128],
                   w=w, i=("c3_wsl", 16), transpose=True)
            w = [("c3_wsl", 16 * 16 * (ph_idx + 1)), ("c2_yev", 64),
                 ("c3_ev", ph_idx)]
            first = True
            for hd in range(NCC):
                for tt8 in range(8):
                    ttile = oct_ * 8 + tt8
                    TE("matmul", c3_ps[:, tt8 * 512:(tt8 + 1) * 512],
                       YT[:, hd * T + ttile * 128: hd * T + ttile * 128 + 128],
                       wp_ring[pr][:, hd * 512:(hd + 1) * 512],
                       start=(hd == 0), stop=(hd == NCC - 1),
                       w=(w if first else ()),
                       i=(("c3_mm", 1) if (hd == NCC - 1 and tt8 == 7) else None))
                    first = False
            VE("tensor_copy", sst_ring[pr][:], c3_ps[:],
               w=[("c3_mm", ph_idx + 1), ("c3_sst", (ph_idx - 1) * 16),
                  ("s_dve", kb.n("s_dve"))],
               i=("c3_ev", 1))
            dst = s_scr[oct_ * 1024:(oct_ + 1) * 1024,
                        ph * 512:(ph + 1) * 512].rearrange(
                            "(tt p) c -> p tt c", p=128)
            src = sst_ring[pr][:].rearrange("p (tt c) -> p tt c", tt=8)
            GQ("dma_start", dst, src, w=[("c3_ev", kb.n("c3_ev"))],
               i=("c3_sst", 16))
            ph_idx += 1
    pc3.close()
    pyt.close()

    if phases < 6:
        fin = [("c3_sst", kb.n("c3_sst"))]
        kb.ops["gpsimd"].append(lambda e, fy=[(kb.sem(s), v) for s, v in fin]:
                                [e.wait_ge(s, v) for s, v in fy])
        return _finish(kb, nc)
    pb2 = ExitStack()
    hT = kb.sb("hT", [128, NIC * 1024], BF16, pb2)
    wd_ring = [kb.sb(f"wdr_{r}", [128, 512], BF16, pb2) for r in range(8)]
    sld_ring = [kb.sb(f"sld_{r}", [128, 8 * 512], BF16, pb2) for r in range(2)]
    x_ring2 = [kb.sb(f"x2_{r}", [128, 512], F32, pb2) for r in range(4)]
    g_ring = [kb.sb(f"g2_{r}", [128, 512], BF16, pb2) for r in range(4)]
    sg_ring = [kb.sb(f"sg_{r}", [128, 512], F32, pb2) for r in range(4)]
    o_ring = [kb.sb(f"o2_{r}", [128, 512], F32, pb2) for r in range(4)]
    b2_ps = kb.ps("b2_ps", [128, 4096], pb2)

    C3_DONE = [("c3_ev", 8), ("c3_sst", 8 * 16), ("c3_mm", 8)]

    if b2lvl < 3:
        for ic in range(NIC):
            w = [("b1_hst", 16 * NIC)]
            if ic == 0:
                w += C3_DONE
            SQ("dma_start", hT[:, ic * 1024:(ic + 1) * 1024],
               h_scr[ic * 128:(ic + 1) * 128, 0:1024],
               w=w, i=("b2_hld", 16))
        sdst = sld_ring[0][:].rearrange("p (tt c) -> p tt c", tt=8)
        ssrc = s_scr[0:1024, 0:512].rearrange("(tt p) c -> p tt c", p=128)
        SQ("dma_start", sdst, ssrc, w=[("c3_sst", 16)] + C3_DONE,
           i=("b2_sld", 16))
        for tt8 in range(8):
            SQ("dma_start", x_ring2[tt8 % 4][:],
               x_d[tt8 * 128:(tt8 + 1) * 128, 0:512],
               w=(C3_DONE if tt8 < 4 else [("b2_hld", 16)]), i=("b2_xld", 16))
        nslab = NIC if b2lvl not in (22, 24, 26) else 8
        for ic in range(nslab):
            w = [("b1_wdst", 16 * 16)]
            if ic < 8:
                w += C3_DONE
            else:
                w.append(("b2_mm8", ic - 7) if b2lvl in (2, 21, 25) else ("b2_xld", 128))
            SQ("dma_start", wd_ring[ic % 8][:],
               wd_s[0:512, ic * 128:(ic + 1) * 128],
               w=w, i=("b2_wsl", 16), transpose=True)
        if b2lvl >= 2:
            nmm = NIC if b2lvl not in (24, 26) else 8
            for ic in range(nmm):
                w = []
                if ic == 0:
                    w = [("b2_hld", 16 * NIC), ("c3_ev", 8)]
                w.append(("b2_wsl", 16 * (min(ic, nslab - 1) + 1)))
                slab = wd_ring[ic % 8] if b2lvl != 22 else wd_ring[0]
                for tt8 in range(8):
                    off = (ic * 1024 + tt8 * 128) if b2lvl != 26 else 0
                    TE("matmul", b2_ps[:, tt8 * 512:(tt8 + 1) * 512],
                       hT[:, off: off + 128],
                       slab[:],
                       start=(ic == 0), stop=(ic == nmm - 1),
                       w=(w if tt8 == 0 else ()),
                       i=(("b2_mm8", 1) if tt8 == 7 else None))
                    w = []
            if b2lvl in (2, 25):
                for tt8 in range(8):
                    SE("activation", g_ring[tt8 % 4][:],
                       b2_ps[:, tt8 * 512:(tt8 + 1) * 512], AF.Silu,
                       w=[("b2_mm8", nmm), ("b2_out", 16 * (tt8 - 3))], i=("b2_g", 1))
                    GQ("dma_start", out_d[tt8 * 128:(tt8 + 1) * 128, 0:512],
                       g_ring[tt8 % 4][:], w=[("b2_g", kb.n("b2_g"))],
                       i=("b2_out", 16))
        fin = [("b2_hld", kb.n("b2_hld")), ("b2_sld", kb.n("b2_sld")),
               ("b2_xld", kb.n("b2_xld")), ("b2_wsl", kb.n("b2_wsl")),
               ("b2_out", kb.n("b2_out")), ("b2_mm8", kb.n("b2_mm8"))]
        kb.ops["gpsimd"].append(lambda e, fy=[(kb.sem(s), v) for s, v in fin]:
                                [e.wait_ge(s, v) for s, v in fy])
        pb2.close()
        return _finish(kb, nc)

    ph_idx = 0
    sl_idx = 0
    NOCT = 1 if b2lvl in (31, 35) else 2
    NPH = 1 if b2lvl in (33, 34, 35) else 4
    for oct_ in range(NOCT):
        for ic in range(NIC):
            w = [("b1_hst", 16 * NIC)]
            if oct_ == 1:
                w.append(("b2_mm8", NPH * NIC))
            if oct_ == 0 and ic == 0:
                w += C3_DONE
            SQ("dma_start", hT[:, ic * 1024:(ic + 1) * 1024],
               h_scr[ic * 128:(ic + 1) * 128, oct_ * 1024:(oct_ + 1) * 1024],
               w=w, i=("b2_hld", 16))
        for ph in range(NPH):
            pr = ph_idx % 2
            w = [("c3_sst", 16 * (oct_ * 4 + ph + 1)),
                 ("b2_comb", (ph_idx - 1) * 8)]
            if ph_idx < 2:
                w += C3_DONE
            sdst = sld_ring[pr][:].rearrange("p (tt c) -> p tt c", tt=8)
            ssrc = s_scr[oct_ * 1024:(oct_ + 1) * 1024,
                         ph * 512:(ph + 1) * 512].rearrange(
                             "(tt p) c -> p tt c", p=128)
            SQ("dma_start", sdst, ssrc, w=w, i=("b2_sld", 16))
            for tt8 in range(8):
                ttile = oct_ * 8 + tt8
                xi = (ph_idx * 8 + tt8) % 4
                w = [("b2_comb", ph_idx * 8 + tt8 - 3)]
                if ph_idx == 0 and tt8 < 4:
                    w += C3_DONE
                GQ("dma_start", x_ring2[xi][:],
                   x_d[ttile * 128:(ttile + 1) * 128, ph * 512:(ph + 1) * 512],
                   w=w, i=("b2_xld", 16))
            for ic in range(NIC):
                w = [("b1_wdst", 16 * 16 * (ph + 1)), ("b2_mm8", sl_idx - 7)]
                if sl_idx < 8:
                    w += C3_DONE
                SQ("dma_start", wd_ring[sl_idx % 8][:],
                   wd_s[ph * 512:(ph + 1) * 512, ic * 128:(ic + 1) * 128],
                   w=w, i=("b2_wsl", 16), transpose=True)
                sl_idx += 1
            base_sl = ph_idx * NIC
            for ic in range(NIC):
                w = []
                if ic == 0:
                    w = [("b2_hld", 16 * NIC * (oct_ + 1)),
                         ("b2_g", ph_idx * 8), ("c3_ev", 8)]
                w.append(("b2_wsl", 16 * (base_sl + ic + 1)))
                for tt8 in range(8):
                    TE("matmul", b2_ps[:, tt8 * 512:(tt8 + 1) * 512],
                       hT[:, ic * 1024 + tt8 * 128: ic * 1024 + tt8 * 128 + 128],
                       wd_ring[(base_sl + ic) % 8][:],
                       start=(ic == 0), stop=(ic == NIC - 1),
                       w=(w if tt8 == 0 else ()),
                       i=(("b2_mm8", 1) if tt8 == 7 else None))
                    w = []
            for tt8 in range(8):
                ttile = oct_ * 8 + tt8
                gi = (ph_idx * 8 + tt8) % 4
                k = ph_idx * 8 + tt8
                SE("activation", g_ring[gi][:],
                   b2_ps[:, tt8 * 512:(tt8 + 1) * 512], AF.Silu,
                   w=[("b2_mm8", (ph_idx + 1) * NIC),
                      (("b2_out", 16 * (k - 3)) if b2lvl == 34 else
                       ("b2_comb", k - 3))],
                   i=("b2_g", 1))
                if b2lvl == 34:
                    GQ("dma_start",
                       out_d[ttile * 128:(ttile + 1) * 128,
                             ph * 512:(ph + 1) * 512],
                       g_ring[gi][:], w=[("b2_g", kb.n("b2_g"))],
                       i=("b2_out", 16))
                    continue
                d0 = kb.n("s_dve")
                VE("tensor_tensor", sg_ring[gi][:], g_ring[gi][:],
                   sld_ring[pr][:, tt8 * 512:(tt8 + 1) * 512], OP.mult,
                   w=[("b2_g", kb.n("b2_g")), ("b2_sld", 16 * (ph_idx + 1)),
                      ("s_dve", d0)],
                   i=("s_dve", 1))
                VE("tensor_tensor", o_ring[gi][:], sg_ring[gi][:], x_ring2[gi][:],
                   OP.add,
                   w=[("b2_xld", 16 * (k + 1)), ("s_dve", kb.n("s_dve")),
                      ("b2_out", 16 * (k - 3))],
                   i=("b2_comb", 1))
                GQ("dma_start",
                   out_d[ttile * 128:(ttile + 1) * 128, ph * 512:(ph + 1) * 512],
                   o_ring[gi][:],
                   w=[("b2_comb", kb.n("b2_comb"))], i=("b2_out", 16))
            ph_idx += 1
    pb2.close()

    fin_sem = kb.sem("b2_out")
    fin_val = kb.n("b2_out")
    kb.ops["gpsimd"].append(lambda e: e.wait_ge(fin_sem, fin_val))
    return _finish(kb, nc)


def _finish(kb, nc):
    with nc.Block() as block:
        @block.sync
        def _(e):
            for f in kb.ops["sync"]:
                f(e)

        @block.gpsimd
        def _(e):
            for f in kb.ops["gpsimd"]:
                f(e)

        @block.vector
        def _(e):
            for f in kb.ops["vector"]:
                f(e)

        @block.scalar
        def _(e):
            for f in kb.ops["scalar"]:
                f(e)

        @block.tensor
        def _(e):
            for f in kb.ops["tensor"]:
                f(e)

    kb.es.close()
    return kb


def _consts():
    inv_freq = 1.0 / (10000.0 ** (np.arange(64, dtype=np.float64) / 64.0))
    pos = np.arange(T, dtype=np.float64)
    ang = inv_freq[:, None] * pos[None, :]
    cos = np.cos(ang)
    sin = np.sin(ang)
    dupc = np.concatenate([cos, cos], axis=0)
    dups = np.concatenate([sin, sin], axis=0)
    sc = 1.0 / np.sqrt(128.0)
    cq = np.ascontiguousarray((dupc * sc).astype(np.float32))
    sq = np.ascontiguousarray((dups * sc).astype(np.float32))
    ck = np.ascontiguousarray(dupc.astype(np.float32))
    sk = np.ascontiguousarray(dups.astype(np.float32))
    ii, jj = np.meshgrid(np.arange(128), np.arange(128), indexing="ij")
    m01 = np.ascontiguousarray((jj >= ii).astype(np.float32))
    ident = np.eye(128, dtype=np.float32)
    return cq, sq, ck, sk, m01, ident


_NC_CACHE = [None]


def kernel(L0_attn_x, L0_attn_Wq, L0_attn_Wk, L0_attn_Wv, L0_attn_Wp,
           L0_ffn_gate_w, L0_ffn_gate_scale, L0_ffn_up_w, L0_ffn_up_scale,
           L0_ffn_down_w, L0_ffn_down_scale):
    x = np.ascontiguousarray(np.asarray(L0_attn_x, dtype=np.float32))
    cq, sq, ck, sk, m01, ident = _consts()
    common = {
        "wq": np.ascontiguousarray(np.asarray(L0_attn_Wq, np.float32)),
        "wk": np.ascontiguousarray(np.asarray(L0_attn_Wk, np.float32)),
        "wv": np.ascontiguousarray(np.asarray(L0_attn_Wv, np.float32)),
        "wp": np.ascontiguousarray(np.asarray(L0_attn_Wp, np.float32)),
        "gw": np.ascontiguousarray(np.asarray(L0_ffn_gate_w, np.int32)),
        "uw": np.ascontiguousarray(np.asarray(L0_ffn_up_w, np.int32)),
        "dw": np.ascontiguousarray(np.asarray(L0_ffn_down_w, np.int32)),
        "gs": np.ascontiguousarray(
            np.asarray(L0_ffn_gate_scale, np.float32).reshape(64, 128)),
        "us": np.ascontiguousarray(
            np.asarray(L0_ffn_up_scale, np.float32).reshape(64, 128)),
        "ds": np.ascontiguousarray(
            np.asarray(L0_ffn_down_scale, np.float32).reshape(16, 128)),
        "cq": cq, "sq": sq, "ck": ck, "sk": sk, "m01": m01, "ident": ident,
    }
    in_maps = [dict(common, x=np.ascontiguousarray(x[b])) for b in range(8)]
    if _NC_CACHE[0] is None:
        _NC_CACHE[0] = build_program()
    nc = _NC_CACHE[0].nc
    res = run_bass_kernel_spmd(nc, in_maps, core_ids=list(range(8)))
    return np.stack([np.asarray(res.results[b]["out"]) for b in range(8)],
                    axis=0)


if __name__ == "__main__":
    build_program()
    print("built ok")
